# version 01
# baseline (speedup 1.0000x reference)
import contextlib

import numpy as np
import concourse.bass as bass
import concourse.mybir as mybir
from concourse.bass_utils import run_bass_kernel_spmd

F32 = mybir.dt.float32
BF16 = mybir.dt.bfloat16
I32 = mybir.dt.int32
AF = mybir.ActivationFunctionType
ALU = mybir.AluOpType

_DBG = False

D = 2048
T = 2048
R = 2048
DH = 512
DI = 1024
NE = 32
NDC = D // 128
NRC = R // 128
NIC = DI // 128
NDHC = DH // 128
TB = 512
NTB = T // TB
GU = 128 * 2048
DD = 128 * 1024
EXP_F = 16 * GU + 16 * DD

ENGINES = ("sync", "tensor", "vector", "scalar", "gpsimd")


def _build_program():
    nc = bass.Bass("TRN2", target_bir_lowering=False)

    xt = nc.dram_tensor("xt", [D, T], F32, kind="ExternalInput").ap()
    wud = nc.dram_tensor("wud", [16, 128, 2048], F32, kind="ExternalInput").ap()
    wvd = nc.dram_tensor("wvd", [16, 128, 2048], F32, kind="ExternalInput").ap()
    wg2d = nc.dram_tensor("wg2d", [16, 128, 512], F32, kind="ExternalInput").ap()
    wg1d = nc.dram_tensor("wg1d", [128, 8192], F32, kind="ExternalInput").ap()
    wmoe = nc.dram_tensor("wmoe", [NE, EXP_F], F32, kind="ExternalInput").ap()
    bup = nc.dram_tensor("bup", [128, 16], F32, kind="ExternalInput").ap()
    bvp = nc.dram_tensor("bvp", [128, 16], F32, kind="ExternalInput").ap()
    bg2p = nc.dram_tensor("bg2p", [128, 16], F32, kind="ExternalInput").ap()
    bg1p = nc.dram_tensor("bg1p", [128, 4], F32, kind="ExternalInput").ap()
    cwp = nc.dram_tensor("cwp", [128, 48], F32, kind="ExternalInput").ap()
    cbp = nc.dram_tensor("cbp", [128, 16], F32, kind="ExternalInput").ap()
    tki = nc.dram_tensor("tki", [4, T], I32, kind="ExternalInput").ap()
    tkw = nc.dram_tensor("tkw", [4, T], F32, kind="ExternalInput").ap()

    outt = nc.dram_tensor("outt", [D, T], F32, kind="ExternalOutput").ap()
    yspill = nc.dram_tensor(
        "yspill", [128, NDC * T], BF16,
        kind="ExternalOutput" if _DBG else "Internal").ap()
    if _DBG:
        d_xn = nc.dram_tensor("d_xn", [128, NDC * T], BF16,
                              kind="ExternalOutput").ap()
        d_y1 = nc.dram_tensor("d_y1", [128, NDC * T], BF16,
                              kind="ExternalOutput").ap()
        d_go = nc.dram_tensor("d_go", [128, NRC * T], BF16,
                              kind="ExternalOutput").ap()

    lists = {e: [] for e in ENGINES}
    cnt = {s: 0 for s in ("sw", "sx", "st", "sv", "ss", "so")}
    waited = {e: {} for e in ENGINES}
    semh = {}

    def op(eng, builder, waits=(), inc=None, amt=1):
        need = tuple((s, v) for (s, v) in waits
                     if v and v > waited[eng].get(s, 0))
        for s, v in need:
            waited[eng][s] = v
        val = None
        if inc is not None:
            cnt[inc] += amt
            val = cnt[inc]

        def f(e, _n=need, _b=builder, _i=inc, _a=amt):
            for s, v in _n:
                e.wait_ge(semh[s], v)
            if _b is not None:
                r = _b(e)
                if _i is not None:
                    r.then_inc(semh[_i], _a)
        lists[eng].append(f)
        return val

    def wait_only(eng, waits):
        op(eng, None, waits=waits)

    def barrier_all():
        b = tuple(cnt.items())
        for e in ENGINES:
            wait_only(e, b)

    def flush_block(name):
        L = {e: lists[e] for e in ENGINES}
        for e in ENGINES:
            lists[e] = []
        with nc.Block(name) as blk:
            @blk.sync
            def _(e):
                for f in L["sync"]:
                    f(e)

            @blk.tensor
            def _(e):
                for f in L["tensor"]:
                    f(e)

            @blk.vector
            def _(e):
                for f in L["vector"]:
                    f(e)

            @blk.scalar
            def _(e):
                for f in L["scalar"]:
                    f(e)

            @blk.gpsimd
            def _(e):
                for f in L["gpsimd"]:
                    f(e)

    with contextlib.ExitStack() as stack:
        ent = stack.enter_context

        for s in ("sw", "sx", "st", "sv", "ss", "so"):
            semh[s] = ent(nc.semaphore(s))

        xn = ent(nc.sbuf_tensor("xn", [128, NDC * T], BF16))
        ybuf = ent(nc.sbuf_tensor("ybuf", [128, NDC * T], BF16))
        ones128 = ent(nc.sbuf_tensor("ones128", [128, 1], BF16))
        ones1 = ent(nc.sbuf_tensor("ones1", [1, 128], BF16))
        ones1f = ent(nc.sbuf_tensor("ones1f", [1, 128], F32))
        ones4 = ent(nc.sbuf_tensor("ones4", [4, 128], BF16))
        zcol = ent(nc.sbuf_tensor("zcol", [128, 1], F32))
        bup_sb = ent(nc.sbuf_tensor("bup_sb", [128, 16], F32))
        bvp_sb = ent(nc.sbuf_tensor("bvp_sb", [128, 16], F32))
        bg2_sb = ent(nc.sbuf_tensor("bg2_sb", [128, 16], F32))
        bg1_sb = ent(nc.sbuf_tensor("bg1_sb", [128, 4], F32))
        cwp_sb = ent(nc.sbuf_tensor("cwp_sb", [128, 48], F32))
        cbp_sb = ent(nc.sbuf_tensor("cbp_sb", [128, 16], F32))
        epsc = ent(nc.sbuf_tensor("epsc", [1, 1], F32))

        psA = ent(nc.psum_tensor("psA", [128, 2048], F32))
        psB = ent(nc.psum_tensor("psB", [128, 2048], F32))
        ps_of = {0: psA, 1: psB}

        with (
            nc.sbuf_tensor("x_stage", [128, 2 * T], F32) as x_stage,
            nc.sbuf_tensor("xbft", [128, T], BF16) as xbft,
            nc.sbuf_tensor("sqbf", [128, T], BF16) as sqbf,
            nc.sbuf_tensor("meanbc", [128, T], F32) as meanbc,
            nc.sbuf_tensor("rstdbc", [128, T], F32) as rstdbc,
            nc.sbuf_tensor("r1", [1, T], F32) as r1,
            nc.sbuf_tensor("r2", [1, T], F32) as r2,
            nc.sbuf_tensor("r3", [1, T], F32) as r3,
            nc.sbuf_tensor("tmp1", [128, T], F32) as tmp1,
        ):
            op("gpsimd", lambda e: e.memset(ones128[:, :], 1.0))
            op("gpsimd", lambda e: e.memset(ones1[:, :], 1.0))
            op("gpsimd", lambda e: e.memset(ones1f[:, :], 1.0))
            op("gpsimd", lambda e: e.memset(ones4[:, :], 1.0))
            op("gpsimd", lambda e: e.memset(zcol[:, :], 0.0))
            op("gpsimd", lambda e: e.memset(epsc[:, :], 1e-5))
            for dst, src in ((bup_sb, bup), (bvp_sb, bvp), (bg2_sb, bg2p),
                             (bg1_sb, bg1p), (cwp_sb, cwp), (cbp_sb, cbp)):
                op("sync", lambda e, d=dst, s=src: e.dma_start(d[:, :], s),
                   inc="sx", amt=16)
            sx_small = cnt["sx"]

            def xdma_op(dc, sl, w):
                return op(
                    "sync",
                    lambda e, dc=dc, sl=sl: e.dma_start(
                        x_stage[:, sl:sl + T], xt[dc * 128:(dc + 1) * 128, :]),
                    waits=w, inc="sx", amt=16)

            xdma = {0: xdma_op(0, 0, []), 1: xdma_op(1, T, [])}
            cvt_done = {}
            mm_done = {}
            for dc in range(NDC):
                sl = (dc % 2) * T
                w = [("sx", xdma[dc])]
                if dc >= 1:
                    w.append(("st", mm_done[dc - 1]))
                op("scalar",
                   lambda e, sl=sl: e.copy(xbft[:, :], x_stage[:, sl:sl + T]),
                   waits=w)
                cvt_done[dc] = op(
                    "scalar",
                    lambda e, sl=sl: e.activation(
                        sqbf[:, :], x_stage[:, sl:sl + T], AF.Square),
                    inc="ss")
                for tb in range(NTB):
                    op("tensor",
                       lambda e, dc=dc, tb=tb: e.matmul(
                           psA[0:1, tb * TB:(tb + 1) * TB], ones128[:, :],
                           xbft[:, tb * TB:(tb + 1) * TB],
                           start=(dc == 0), stop=(dc == NDC - 1)),
                       waits=[("ss", cvt_done[dc])])
                for tb in range(NTB):
                    mm_done[dc] = op(
                        "tensor",
                        lambda e, dc=dc, tb=tb: e.matmul(
                            psB[0:1, tb * TB:(tb + 1) * TB], ones128[:, :],
                            sqbf[:, tb * TB:(tb + 1) * TB],
                            start=(dc == 0), stop=(dc == NDC - 1)),
                        inc="st" if tb == NTB - 1 else None)
                if dc + 2 < NDC:
                    xdma[dc + 2] = xdma_op(dc + 2, (dc % 2) * T,
                                           [("ss", cvt_done[dc])])

            stats_done = mm_done[NDC - 1]
            inv = 1.0 / D
            v_mean = op("vector",
                        lambda e: e.tensor_scalar_mul(r1[:, :], psA[0:1, :],
                                                      inv),
                        waits=[("st", stats_done)], inc="sv")
            op("vector", lambda e: e.tensor_mul(r2[:, :], r1[:, :], r1[:, :]))
            v_var = op("vector",
                       lambda e: e.scalar_tensor_tensor(
                           r2[:, :], psB[0:1, :], inv, r2[:, :],
                           ALU.mult, ALU.subtract),
                       inc="sv")
            s_sq = op("scalar",
                      lambda e: e.activation(r3[:, :], r2[:, :], AF.Sqrt,
                                             bias=epsc[0:1, :]),
                      waits=[("sv", v_var)], inc="ss")
            v_rec = op("vector", lambda e: e.reciprocal(r2[:, :], r3[:, :]),
                       waits=[("ss", s_sq)], inc="sv")
            bc_done = None
            for tb in range(NTB):
                op("tensor",
                   lambda e, tb=tb: e.matmul(
                       psA[:, tb * TB:(tb + 1) * TB], ones1f[:, :],
                       r1[0:1, tb * TB:(tb + 1) * TB], start=True, stop=True),
                   waits=[("sv", v_rec)])
            for tb in range(NTB):
                bc_done = op(
                    "tensor",
                    lambda e, tb=tb: e.matmul(
                        psB[:, tb * TB:(tb + 1) * TB], ones1f[:, :],
                        r2[0:1, tb * TB:(tb + 1) * TB], start=True, stop=True),
                    inc="st" if tb == NTB - 1 else None)
            s_mean = op("scalar", lambda e: e.copy(meanbc[:, :], psA[:, :]),
                        waits=[("st", bc_done)], inc="ss")
            v_rstd = op("vector", lambda e: e.tensor_copy(rstdbc[:, :],
                                                          psB[:, :]),
                        waits=[("st", bc_done)], inc="sv")

            w0 = [("ss", s_mean), ("sv", v_rstd)]
            xdma2 = {0: xdma_op(0, 0, w0), 1: xdma_op(1, T, w0)}
            bdone = {}
            edge_done = {}
            for dc in range(NDC):
                sl = (dc % 2) * T
                w = [("sx", xdma2[dc]), ("ss", s_mean), ("sv", v_rstd)]
                if dc >= 1:
                    w.append(("ss", edge_done[dc - 1]))
                op("vector",
                   lambda e, sl=sl: e.tensor_sub(
                       tmp1[:, :], x_stage[:, sl:sl + T], meanbc[:, :]),
                   waits=w)
                xn_dc = op(
                    "vector",
                    lambda e, dc=dc: e.tensor_mul(
                        xn[:, dc * T:(dc + 1) * T], tmp1[:, :], rstdbc[:, :]),
                    inc="sv")
                s_c1 = op("scalar",
                          lambda e, dc=dc: e.activation(
                              tmp1[:, :], xn[:, dc * T:(dc + 1) * T], AF.Identity,
                              bias=cbp_sb[:, dc:dc + 1],
                              scale=cwp_sb[:, dc * 3 + 1:dc * 3 + 2]),
                          waits=[("sv", xn_dc), ("sx", sx_small)], inc="ss")
                op("vector",
                   lambda e, dc=dc: e.scalar_tensor_tensor(
                       tmp1[:, 1:T], xn[:, dc * T:dc * T + T - 1],
                       cwp_sb[:, dc * 3:dc * 3 + 1], tmp1[:, 1:T],
                       ALU.mult, ALU.add),
                   waits=[("ss", s_c1)])
                bdone[dc] = op(
                    "vector",
                    lambda e, dc=dc: e.scalar_tensor_tensor(
                        ybuf[:, dc * T:dc * T + T - 1],
                        xn[:, dc * T + 1:(dc + 1) * T],
                        cwp_sb[:, dc * 3 + 2:dc * 3 + 3], tmp1[:, 0:T - 1],
                        ALU.mult, ALU.add),
                    inc="sv")
                edge_done[dc] = op(
                    "scalar",
                    lambda e, dc=dc: e.copy(
                        ybuf[:, dc * T + T - 1:(dc + 1) * T],
                        tmp1[:, T - 1:T]),
                    waits=[("sv", bdone[dc])], inc="ss")
                if dc + 2 < NDC:
                    xdma2[dc + 2] = xdma_op(dc + 2, (dc % 2) * T,
                                            [("sv", bdone[dc])])
            xn_ready = cnt["sv"]
            convs_done = cnt["ss"]
            if _DBG:
                op("sync", lambda e: e.dma_start(d_xn, xn[:, :]),
                   waits=[("sv", xn_ready)], inc="so", amt=16)
                op("sync", lambda e: e.dma_start(d_y1, ybuf[:, :]),
                   waits=[("sv", xn_ready), ("ss", convs_done)],
                   inc="so", amt=16)
            flush_block("ln")

        barrier_all()
        with nc.sbuf_tensor("wring", [128, 4 * 2048], BF16) as wring:
            slot_free = [("st", 0)] * 4
            blk_i = [0]

            def wblock_dma(src_ap, waits=()):
                j = blk_i[0] % 4
                width = src_ap.shape[-1] if src_ap.ndim == 2 else None
                blk_i[0] += 1
                dst = wring[:, j * 2048: j * 2048 + src_ap.shape[-1]] \
                    if src_ap.ndim == 2 else None
                v = op("gpsimd",
                       lambda e, dst=dst, src=src_ap: e.dma_start(dst, src),
                       waits=[slot_free[j]] + list(waits), inc="sw", amt=16)
                return v, j

            def wflat_dma(row, off, width):
                j = blk_i[0] % 4
                blk_i[0] += 1
                dst = wring[:, j * 2048: j * 2048 + width]
                src = wmoe[row, off: off + 128 * width].rearrange(
                    "(p w) -> p w", w=width)
                v = op("gpsimd",
                       lambda e, dst=dst, src=src: e.dma_start(dst, src),
                       waits=[slot_free[j]], inc="sw", amt=16)
                return v, j

            with (
                nc.sbuf_tensor("hmid", [128, NIC * T], BF16) as hmid,
                nc.sbuf_tensor("maskbc", [128, T], BF16) as maskbc,
                nc.sbuf_tensor("silu_t", [128, T], BF16) as silu_t,
                nc.sbuf_tensor("m4", [4, T], BF16) as m4,
                nc.sbuf_tensor("idxi", [4, T], I32) as idxi,
                nc.sbuf_tensor("tkw_bf", [4, T], BF16) as tkw_bf,
            ):
                d_tki = op("sync", lambda e: e.dma_start(idxi[:, :], tki),
                           inc="sx", amt=16)
                d_tkw = op("gpsimd", lambda e: e.dma_start(tkw_bf[:, :], tkw),
                           inc="sw", amt=16)

                psA_free = ("sv", 0)
                psB_free = ("ss", 0)
                hmid_rd = ("st", 0)
                silu_free = ("sv", 0)
                prev_mask_mm = 0
                for e_i in range(NE):
                    v_m4 = op("vector",
                              lambda e, e_i=e_i: e.scalar_tensor_tensor(
                                  m4[:, :], idxi[:, :], float(e_i),
                                  tkw_bf[:, :], ALU.is_equal, ALU.mult),
                              waits=[("sx", d_tki), ("sw", d_tkw),
                                     ("st", prev_mask_mm)],
                              inc="sv")
                    mask_mm = None
                    for tb in range(NTB):
                        mask_mm = op(
                            "tensor",
                            lambda e, tb=tb: e.matmul(
                                psA[:, tb * TB:(tb + 1) * TB], ones4[:, :],
                                m4[0:4, tb * TB:(tb + 1) * TB],
                                start=True, stop=True),
                            waits=[("sv", v_m4), psA_free],
                            inc="st" if tb == NTB - 1 else None)
                    prev_mask_mm = mask_mm
                    s_mask = op("scalar",
                                lambda e: e.copy(maskbc[:, :], psA[:, :]),
                                waits=[("st", mask_mm)], inc="ss")
                    psA_free = ("ss", s_mask)

                    hmid_done = {}
                    for ic in range(NIC):
                        wg, jg = wflat_dma(e_i, ic * 2 * GU, 2048)
                        wu_, ju = wflat_dma(e_i, ic * 2 * GU + GU, 2048)
                        g_last = None
                        for hc in range(NDC):
                            for tb in range(NTB):
                                g_last = op(
                                    "tensor",
                                    lambda e, jg=jg, hc=hc, tb=tb: e.matmul(
                                        psB[:, tb * TB:(tb + 1) * TB],
                                        wring[:, jg * 2048 + hc * 128:
                                              jg * 2048 + (hc + 1) * 128],
                                        xn[:, hc * T + tb * TB:
                                           hc * T + (tb + 1) * TB],
                                        start=(hc == 0), stop=(hc == NDC - 1)),
                                    waits=[("sw", wg), psB_free],
                                    inc="st" if (hc == NDC - 1 and
                                                 tb == NTB - 1) else None)
                        slot_free[jg] = ("st", g_last)
                        s_silu = op(
                            "scalar",
                            lambda e: e.activation(silu_t[:, :], psB[:, :],
                                                   AF.Silu),
                            waits=[("st", g_last), silu_free], inc="ss")
                        psB_free = ("ss", s_silu)
                        v_sgm = op("vector",
                                   lambda e: e.tensor_mul(
                                       silu_t[:, :], silu_t[:, :],
                                       maskbc[:, :]),
                                   waits=[("ss", s_silu)], inc="sv")
                        u_last = None
                        for hc in range(NDC):
                            for tb in range(NTB):
                                u_last = op(
                                    "tensor",
                                    lambda e, ju=ju, hc=hc, tb=tb: e.matmul(
                                        psA[:, tb * TB:(tb + 1) * TB],
                                        wring[:, ju * 2048 + hc * 128:
                                              ju * 2048 + (hc + 1) * 128],
                                        xn[:, hc * T + tb * TB:
                                           hc * T + (tb + 1) * TB],
                                        start=(hc == 0), stop=(hc == NDC - 1)),
                                    waits=[("sw", wu_), psA_free],
                                    inc="st" if (hc == NDC - 1 and
                                                 tb == NTB - 1) else None)
                        slot_free[ju] = ("st", u_last)
                        hmid_done[ic] = op(
                            "vector",
                            lambda e, ic=ic: e.tensor_mul(
                                hmid[:, ic * T:(ic + 1) * T], psA[:, :],
                                silu_t[:, :]),
                            waits=[("st", u_last), ("sv", v_sgm), hmid_rd],
                            inc="sv")
                        psA_free = ("sv", hmid_done[ic])
                        silu_free = ("sv", hmid_done[ic])

                    o_done = {}
                    y_add = {}
                    for hc in range(NDC):
                        wd, jd = wflat_dma(e_i, 16 * GU + hc * DD, 1024)
                        ps = ps_of[hc % 2]
                        free = psB_free if hc % 2 == 1 else psA_free
                        w = [("sw", wd), free, ("sv", hmid_done[NIC - 1])]
                        for ic in range(NIC):
                            for tb in range(NTB):
                                o_done[hc] = op(
                                    "tensor",
                                    lambda e, ps=ps, jd=jd, ic=ic, tb=tb:
                                    e.matmul(
                                        ps[:, tb * TB:(tb + 1) * TB],
                                        wring[:, jd * 2048 + ic * 128:
                                              jd * 2048 + (ic + 1) * 128],
                                        hmid[:, ic * T + tb * TB:
                                             ic * T + (tb + 1) * TB],
                                        start=(ic == 0), stop=(ic == NIC - 1)),
                                    waits=w,
                                    inc="st" if (ic == NIC - 1 and
                                                 tb == NTB - 1) else None)
                        slot_free[jd] = ("st", o_done[hc])
                        y_add[hc] = op(
                            "vector",
                            lambda e, ps=ps, hc=hc: e.tensor_tensor(
                                ybuf[:, hc * T:(hc + 1) * T], ps[:, :],
                                ybuf[:, hc * T:(hc + 1) * T], ALU.add),
                            waits=[("st", o_done[hc])], inc="sv")
                        if hc % 2 == 1:
                            psB_free = ("sv", y_add[hc])
                        else:
                            psA_free = ("sv", y_add[hc])
                    hmid_rd = ("st", o_done[NDC - 1])
                moe_y_done = cnt["sv"]
                flush_block("moe")

            barrier_all()
            with (
                nc.sbuf_tensor("wg1_bf", [128, 8192], BF16) as wg1_bf,
                nc.sbuf_tensor("h_all", [128, NDHC * T], BF16) as h_all,
                nc.sbuf_tensor("gates", [128, T], F32) as gates,
                nc.sbuf_tensor("ug", [128, T], F32) as ug,
                nc.sbuf_tensor("vg", [128, T], F32) as vg,
            ):
                sp = op("sync", lambda e: e.dma_start(yspill, ybuf[:, :]),
                        waits=[("sv", moe_y_done)], inc="sx", amt=16)
                d_wg1 = op("gpsimd", lambda e: e.dma_start(wg1_bf[:, :], wg1d),
                           inc="sw", amt=16)
                h_done = {}
                relu_done = {}
                for dhc in range(NDHC):
                    ps = ps_of[dhc % 2]
                    for dc in range(NDC):
                        for tb in range(NTB):
                            h_done[dhc] = op(
                                "tensor",
                                lambda e, ps=ps, dc=dc, dhc=dhc, tb=tb:
                                e.matmul(
                                    ps[:, tb * TB:(tb + 1) * TB],
                                    wg1_bf[:, dc * 512 + dhc * 128:
                                           dc * 512 + (dhc + 1) * 128],
                                    xn[:, dc * T + tb * TB:
                                       dc * T + (tb + 1) * TB],
                                    start=(dc == 0), stop=(dc == NDC - 1)),
                                waits=[("sw", d_wg1),
                                       ("ss", relu_done.get(dhc - 2, 0))],
                                inc="st" if (dc == NDC - 1 and tb == NTB - 1)
                                else None)
                    relu_done[dhc] = op(
                        "scalar",
                        lambda e, ps=ps, dhc=dhc: e.activation(
                            h_all[:, dhc * T:(dhc + 1) * T], ps[:, :], AF.Relu,
                            bias=bg1_sb[:, dhc:dhc + 1]),
                        waits=[("st", h_done[dhc])], inc="ss")
                h_ready = relu_done[NDHC - 1]

                psA_free = ("ss", relu_done.get(NDHC - 2, 0))
                psB_free = ("ss", h_ready)
                go_done = {}
                for rc in range(NRC):
                    v2, jg2 = wblock_dma(wg2d[rc])
                    vu, ju_ = wblock_dma(wud[rc])
                    vv, jv_ = wblock_dma(wvd[rc])

                    gmm = None
                    for dhc in range(NDHC):
                        for tb in range(NTB):
                            gmm = op(
                                "tensor",
                                lambda e, jg2=jg2, dhc=dhc, tb=tb: e.matmul(
                                    psB[:, tb * TB:(tb + 1) * TB],
                                    wring[:, jg2 * 2048 + dhc * 128:
                                          jg2 * 2048 + (dhc + 1) * 128],
                                    h_all[:, dhc * T + tb * TB:
                                          dhc * T + (tb + 1) * TB],
                                    start=(dhc == 0), stop=(dhc == NDHC - 1)),
                                waits=[("sw", v2), ("ss", h_ready), psB_free],
                                inc="st" if (dhc == NDHC - 1 and tb == NTB - 1)
                                else None)
                    slot_free[jg2] = ("st", gmm)
                    s_gate = op(
                        "scalar",
                        lambda e, rc=rc: e.activation(
                            gates[:, :], psB[:, :], AF.Sigmoid,
                            bias=bg2_sb[:, rc:rc + 1]),
                        waits=[("st", gmm)], inc="ss")
                    psB_free = ("ss", s_gate)
                    umm = None
                    for dc in range(NDC):
                        for tb in range(NTB):
                            umm = op(
                                "tensor",
                                lambda e, ju_=ju_, dc=dc, tb=tb: e.matmul(
                                    psA[:, tb * TB:(tb + 1) * TB],
                                    wring[:, ju_ * 2048 + dc * 128:
                                          ju_ * 2048 + (dc + 1) * 128],
                                    xn[:, dc * T + tb * TB:
                                       dc * T + (tb + 1) * TB],
                                    start=(dc == 0), stop=(dc == NDC - 1)),
                                waits=[("sw", vu), psA_free],
                                inc="st" if (dc == NDC - 1 and tb == NTB - 1)
                                else None)
                    slot_free[ju_] = ("st", umm)
                    v_ug = op("vector",
                              lambda e, rc=rc: e.scalar_tensor_tensor(
                                  ug[:, :], psA[:, :], bup_sb[:, rc:rc + 1],
                                  gates[:, :], ALU.add, ALU.mult),
                              waits=[("st", umm), ("ss", s_gate)], inc="sv")
                    psA_free = ("sv", v_ug)
                    vmm = None
                    for dc in range(NDC):
                        for tb in range(NTB):
                            vmm = op(
                                "tensor",
                                lambda e, jv_=jv_, dc=dc, tb=tb: e.matmul(
                                    psB[:, tb * TB:(tb + 1) * TB],
                                    wring[:, jv_ * 2048 + dc * 128:
                                          jv_ * 2048 + (dc + 1) * 128],
                                    xn[:, dc * T + tb * TB:
                                       dc * T + (tb + 1) * TB],
                                    start=(dc == 0), stop=(dc == NDC - 1)),
                                waits=[("sw", vv), psB_free],
                                inc="st" if (dc == NDC - 1 and tb == NTB - 1)
                                else None)
                    slot_free[jv_] = ("st", vmm)
                    v_vg = op("vector",
                              lambda e, rc=rc: e.scalar_tensor_tensor(
                                  vg[:, :], psB[:, :], bvp_sb[:, rc:rc + 1],
                                  gates[:, :], ALU.add, ALU.mult),
                              waits=[("st", vmm)], inc="sv")
                    psB_free = ("sv", v_vg)
                    op("vector",
                       lambda e: e.tensor_tensor_scan(
                           ug[:, :], ug[:, :],
                           zcol[:, :].broadcast_to((128, T)), 0.0,
                           ALU.add, ALU.add))
                    go_done[rc] = op(
                        "vector",
                        lambda e, rc=rc: e.tensor_mul(
                            ybuf[:, rc * T:(rc + 1) * T], ug[:, :], vg[:, :]),
                        waits=[("sx", sp)], inc="sv")
                go_ready = go_done[NRC - 1]
                if _DBG:
                    op("sync", lambda e: e.dma_start(d_go, ybuf[:, :]),
                       waits=[("sv", go_ready)], inc="so", amt=16)
                flush_block("attn")

            barrier_all()
            with (
                nc.sbuf_tensor("x_stage2", [128, 2 * T], F32) as x_stage2,
                nc.sbuf_tensor("ystr", [128, 2 * T], BF16) as ystr,
                nc.sbuf_tensor("ostg", [128, 2 * T], F32) as ostg,
                nc.sbuf_tensor("ftmp", [128, T], F32) as ftmp,
            ):
                xl = {}
                yl = {}
                out_done = {}
                vsum = {}
                for dc in range(NDC):
                    sl = (dc % 2) * T
                    w = [] if dc < 2 else [("sv", vsum[dc - 2])]
                    xl[dc] = op("sync",
                                lambda e, dc=dc, sl=sl: e.dma_start(
                                    x_stage2[:, sl:sl + T],
                                    xt[dc * 128:(dc + 1) * 128, :]),
                                waits=w, inc="sx", amt=16)
                    yl[dc] = op("sync",
                                lambda e, dc=dc, sl=sl: e.dma_start(
                                    ystr[:, sl:sl + T],
                                    yspill[:, dc * T:(dc + 1) * T]),
                                inc="sx", amt=16)
                    if dc >= 2:
                        out_done[dc - 2] = op(
                            "sync",
                            lambda e, dc=dc, sl=sl: e.dma_start(
                                outt[(dc - 2) * 128:(dc - 1) * 128, :],
                                ostg[:, sl:sl + T]),
                            waits=[("sv", vsum[dc - 2])], inc="so", amt=16)
                    vw, jw = wblock_dma(wud[dc])
                    ps = ps_of[dc % 2]
                    uv_last = None
                    for rc in range(NRC):
                        for tb in range(NTB):
                            uv_last = op(
                                "tensor",
                                lambda e, ps=ps, jw=jw, rc=rc, tb=tb: e.matmul(
                                    ps[:, tb * TB:(tb + 1) * TB],
                                    wring[:, jw * 2048 + rc * 128:
                                          jw * 2048 + (rc + 1) * 128],
                                    ybuf[:, rc * T + tb * TB:
                                         rc * T + (tb + 1) * TB],
                                    start=(rc == 0), stop=(rc == NRC - 1)),
                                waits=[("sw", vw),
                                       ("sv", vsum.get(dc - 2, 0))],
                                inc="st" if (rc == NRC - 1 and tb == NTB - 1)
                                else None)
                    slot_free[jw] = ("st", uv_last)
                    op("vector",
                       lambda e, ps=ps, sl=sl: e.tensor_add(
                           ftmp[:, :], ps[:, :], x_stage2[:, sl:sl + T]),
                       waits=[("st", uv_last), ("sx", xl[dc]), ("sx", yl[dc]),
                              ("so", out_done.get(dc - 2, 0))])
                    vsum[dc] = op(
                        "vector",
                        lambda e, sl=sl: e.tensor_add(
                            ostg[:, sl:sl + T], ftmp[:, :],
                            ystr[:, sl:sl + T]),
                        inc="sv")
                for dc in (NDC - 2, NDC - 1):
                    sl = (dc % 2) * T
                    out_done[dc] = op(
                        "sync",
                        lambda e, dc=dc, sl=sl: e.dma_start(
                            outt[dc * 128:(dc + 1) * 128, :],
                            ostg[:, sl:sl + T]),
                        waits=[("sv", vsum[dc])], inc="so", amt=16)
                wait_only("sync", [("so", cnt["so"])])
                flush_block("final")

    return nc


def _pack_t4(a):
    n = a.shape[0] // 128
    c = a.shape[1] // 128
    return np.ascontiguousarray(
        a.reshape(n, 128, c, 128).transpose(2, 1, 0, 3).reshape(
            c, 128, n * 128))


def kernel(L0_attn_x, L0_attn_Wu, L0_attn_bu, L0_attn_Wv, L0_attn_bv,
           L0_attn_Wg1, L0_attn_bg1, L0_attn_Wg2, L0_attn_bg2,
           L0_attn_conv_w, L0_attn_conv_b,
           L0_ffn_gate_weight, L0_ffn_up_weight, L0_ffn_down_weight,
           L0_ffn_topk_idx, L0_ffn_topk_weight):
    x = np.asarray(L0_attn_x, dtype=np.float32)
    Wu = np.asarray(L0_attn_Wu, dtype=np.float32)
    Wv = np.asarray(L0_attn_Wv, dtype=np.float32)
    Wg1 = np.asarray(L0_attn_Wg1, dtype=np.float32)
    Wg2 = np.asarray(L0_attn_Wg2, dtype=np.float32)
    bu = np.asarray(L0_attn_bu, dtype=np.float32)
    bv = np.asarray(L0_attn_bv, dtype=np.float32)
    bg1 = np.asarray(L0_attn_bg1, dtype=np.float32)
    bg2 = np.asarray(L0_attn_bg2, dtype=np.float32)
    convw = np.asarray(L0_attn_conv_w, dtype=np.float32)
    convb = np.asarray(L0_attn_conv_b, dtype=np.float32)
    gw = np.asarray(L0_ffn_gate_weight, dtype=np.float32)
    uw = np.asarray(L0_ffn_up_weight, dtype=np.float32)
    dw = np.asarray(L0_ffn_down_weight, dtype=np.float32)
    tidx = np.asarray(L0_ffn_topk_idx, dtype=np.int32)
    tw = np.asarray(L0_ffn_topk_weight, dtype=np.float32)

    B = x.shape[0]

    wud = _pack_t4(np.ascontiguousarray(Wu.T))
    wvd = _pack_t4(np.ascontiguousarray(Wv.T))
    wg2d = _pack_t4(np.ascontiguousarray(Wg2.T))
    wg1d = np.ascontiguousarray(
        Wg1.T.reshape(16, 128, 512).transpose(1, 0, 2).reshape(128, 8192))
    moe_blocks = []
    for e in range(NE):
        gt = _pack_t4(np.ascontiguousarray(gw[e].T))
        ut = _pack_t4(np.ascontiguousarray(uw[e].T))
        dt = _pack_t4(np.ascontiguousarray(dw[e].T))
        parts = []
        for ic in range(NIC):
            parts.append(gt[ic].ravel())
            parts.append(ut[ic].ravel())
        for hc in range(NDC):
            parts.append(dt[hc].ravel())
        moe_blocks.append(np.concatenate(parts))
    wmoe_np = np.ascontiguousarray(np.stack(moe_blocks))
    del moe_blocks

    bup_np = np.ascontiguousarray(bu.reshape(16, 128).T)
    bvp_np = np.ascontiguousarray(bv.reshape(16, 128).T)
    bg2_np = np.ascontiguousarray(bg2.reshape(16, 128).T)
    bg1_np = np.ascontiguousarray(bg1.reshape(4, 128).T)
    cwp_np = np.ascontiguousarray(
        convw[:, 0, :].reshape(16, 128, 3).transpose(1, 0, 2).reshape(128, 48))
    cbp_np = np.ascontiguousarray(convb.reshape(16, 128).T)

    nc = _build_program()
    in_maps = []
    for b in range(B):
        in_maps.append({
            "xt": np.ascontiguousarray(x[b].T),
            "wud": wud, "wvd": wvd, "wg2d": wg2d, "wg1d": wg1d,
            "wmoe": wmoe_np,
            "bup": bup_np, "bvp": bvp_np, "bg2p": bg2_np, "bg1p": bg1_np,
            "cwp": cwp_np, "cbp": cbp_np,
            "tki": np.ascontiguousarray(tidx[b * T:(b + 1) * T].T),
            "tkw": np.ascontiguousarray(tw[b * T:(b + 1) * T].T),
        })
    res = run_bass_kernel_spmd(nc, in_maps, core_ids=list(range(B)))
    if _DBG:
        kernel._dbg = res.results
    out = np.empty((B, T, D), dtype=np.float32)
    for b in range(B):
        out[b] = np.asarray(res.results[b]["outt"]).T
    return out


# version 02
# speedup vs baseline: 1.6868x; 1.6868x over previous
import contextlib

import numpy as np
import concourse.bass as bass
import concourse.mybir as mybir
from concourse.bass_utils import run_bass_kernel_spmd

F32 = mybir.dt.float32
BF16 = mybir.dt.bfloat16
I32 = mybir.dt.int32
AF = mybir.ActivationFunctionType
ALU = mybir.AluOpType

_DBG = False

D = 2048
T = 2048
R = 2048
DH = 512
DI = 1024
NE = 32
NK = 4
CAP = 512
NCT = CAP // 128
NDC = D // 128
NRC = R // 128
NIC = DI // 128
NDHC = DH // 128
TB = 512
NTB = T // TB
GU = 128 * 2048
DD = 128 * 1024
EXP_F = 16 * GU + 16 * DD
NSLOT = T * NK
PAD = 8192.0

ENGINES = ("sync", "tensor", "vector", "scalar", "gpsimd")


def _build_program():
    nc = bass.Bass("TRN2", target_bir_lowering=False)

    xt = nc.dram_tensor("xt", [D, T], F32, kind="ExternalInput").ap()
    wud = nc.dram_tensor("wud", [16, 128, 2048], F32, kind="ExternalInput").ap()
    wvd = nc.dram_tensor("wvd", [16, 128, 2048], F32, kind="ExternalInput").ap()
    wg2d = nc.dram_tensor("wg2d", [16, 128, 512], F32, kind="ExternalInput").ap()
    wg1d = nc.dram_tensor("wg1d", [128, 8192], F32, kind="ExternalInput").ap()
    wmoe = nc.dram_tensor("wmoe", [NE, EXP_F], F32, kind="ExternalInput").ap()
    bup = nc.dram_tensor("bup", [128, 16], F32, kind="ExternalInput").ap()
    bvp = nc.dram_tensor("bvp", [128, 16], F32, kind="ExternalInput").ap()
    bg2p = nc.dram_tensor("bg2p", [128, 16], F32, kind="ExternalInput").ap()
    bg1p = nc.dram_tensor("bg1p", [128, 4], F32, kind="ExternalInput").ap()
    cwp = nc.dram_tensor("cwp", [128, 48], F32, kind="ExternalInput").ap()
    cbp = nc.dram_tensor("cbp", [128, 16], F32, kind="ExternalInput").ap()
    tki = nc.dram_tensor("tki", [128, 64], I32, kind="ExternalInput").ap()
    tkw = nc.dram_tensor("tkw", [128, 64], F32, kind="ExternalInput").ap()
    tokid = nc.dram_tensor("tokid", [128, 64], F32, kind="ExternalInput").ap()
    srowid = nc.dram_tensor("srowid", [128, 64], F32, kind="ExternalInput").ap()
    ltri = nc.dram_tensor("ltri", [128, 128], F32, kind="ExternalInput").ap()

    outt = nc.dram_tensor("outt", [D, T], F32, kind="ExternalOutput").ap()
    conv_h = nc.dram_tensor("conv_h", [128, NDC * T], BF16, kind="Internal").ap()
    xn_fm = nc.dram_tensor("xn_fm", [D, T], BF16, kind="Internal").ap()
    xn_tm = nc.dram_tensor("xn_tm", [T, D], BF16,
                           kind="ExternalOutput" if _DBG else "Internal").ap()
    xg_h = nc.dram_tensor("xg_h", [NE * CAP, D], BF16,
                          kind="ExternalOutput" if _DBG else "Internal").ap()
    o_h = nc.dram_tensor("o_h", [NE, D, CAP], BF16,
                         kind="ExternalOutput" if _DBG else "Internal").ap()
    y4 = nc.dram_tensor("y4", [NSLOT + 128, D], BF16,
                        kind="ExternalOutput" if _DBG else "Internal").ap()
    glist = nc.dram_tensor("glist", [NE * CAP, 4], F32,
                           kind="ExternalOutput" if _DBG else "Internal").ap()
    if _DBG:
        d_xn = nc.dram_tensor("d_xn", [128, NDC * T], BF16,
                              kind="ExternalOutput").ap()
        d_y1 = nc.dram_tensor("d_y1", [128, NDC * T], BF16,
                              kind="ExternalOutput").ap()
        d_go = nc.dram_tensor("d_go", [128, NRC * T], BF16,
                              kind="ExternalOutput").ap()

    lists = {e: [] for e in ENGINES}
    cnt = {s: 0 for s in ("sw", "sx", "sq", "st", "sv", "ss", "so")}
    waited = {e: {} for e in ENGINES}
    semh = {}

    def op(eng, builder, waits=(), inc=None, amt=1):
        need = tuple((s, v) for (s, v) in waits
                     if v and v > waited[eng].get(s, 0))
        for s, v in need:
            waited[eng][s] = v
        val = None
        if inc is not None:
            cnt[inc] += amt
            val = cnt[inc]

        def f(e, _n=need, _b=builder, _i=inc, _a=amt):
            for s, v in _n:
                e.wait_ge(semh[s], v)
            if _b is not None:
                r = _b(e)
                if _i is not None:
                    r.then_inc(semh[_i], _a)
        lists[eng].append(f)
        return val

    def wait_only(eng, waits):
        op(eng, None, waits=waits)

    def barrier_all():
        b = tuple(cnt.items())
        for e in ENGINES:
            wait_only(e, b)

    def flush_block(name):
        L = {e: lists[e] for e in ENGINES}
        for e in ENGINES:
            lists[e] = []
        with nc.Block(name) as blk:
            @blk.sync
            def _(e):
                for f in L["sync"]:
                    f(e)

            @blk.tensor
            def _(e):
                for f in L["tensor"]:
                    f(e)

            @blk.vector
            def _(e):
                for f in L["vector"]:
                    f(e)

            @blk.scalar
            def _(e):
                for f in L["scalar"]:
                    f(e)

            @blk.gpsimd
            def _(e):
                for f in L["gpsimd"]:
                    f(e)

    with contextlib.ExitStack() as stack:
        ent = stack.enter_context

        for s in ("sw", "sx", "sq", "st", "sv", "ss", "so"):
            semh[s] = ent(nc.semaphore(s))

        xn = ent(nc.sbuf_tensor("xn", [128, NDC * T], BF16))
        ones128 = ent(nc.sbuf_tensor("ones128", [128, 1], BF16))
        ones1f = ent(nc.sbuf_tensor("ones1f", [1, 128], F32))
        zcol = ent(nc.sbuf_tensor("zcol", [128, 1], F32))
        bup_sb = ent(nc.sbuf_tensor("bup_sb", [128, 16], F32))
        bvp_sb = ent(nc.sbuf_tensor("bvp_sb", [128, 16], F32))
        bg2_sb = ent(nc.sbuf_tensor("bg2_sb", [128, 16], F32))
        bg1_sb = ent(nc.sbuf_tensor("bg1_sb", [128, 4], F32))
        cwp_sb = ent(nc.sbuf_tensor("cwp_sb", [128, 48], F32))
        cbp_sb = ent(nc.sbuf_tensor("cbp_sb", [128, 16], F32))
        epsc = ent(nc.sbuf_tensor("epsc", [1, 1], F32))
        ltri_sb = ent(nc.sbuf_tensor("ltri_sb", [128, 128], F32))

        psA = ent(nc.psum_tensor("psA", [128, 2048], F32))
        psB = ent(nc.psum_tensor("psB", [128, 2048], F32))
        ps_of = {0: psA, 1: psB}
        bank = [(psA, q) for q in range(4)] + [(psB, q) for q in range(4)]

        def bank_ap(i):
            t, q = bank[i]
            return t[:, q * TB:(q + 1) * TB]

        with (
            nc.sbuf_tensor("ybuf", [128, NDC * T], BF16) as ybuf,
            nc.sbuf_tensor("idxi", [128, 64], I32) as idxi,
            nc.sbuf_tensor("tkw_sb", [128, 64], F32) as tkw_sb,
            nc.sbuf_tensor("tok_sb", [128, 64], F32) as tok_sb,
            nc.sbuf_tensor("srw_sb", [128, 64], F32) as srw_sb,
            nc.sbuf_tensor("padt", [128, 512], F32) as padt,
        ):
            ln_scope = contextlib.ExitStack()
            x_stage = ln_scope.enter_context(
                nc.sbuf_tensor("x_stage", [128, 2 * T], F32))
            xbft = ln_scope.enter_context(
                nc.sbuf_tensor("xbft", [128, T], BF16))
            sqbf = ln_scope.enter_context(
                nc.sbuf_tensor("sqbf", [128, T], BF16))
            meanbc = ln_scope.enter_context(
                nc.sbuf_tensor("meanbc", [128, T], F32))
            rstdbc = ln_scope.enter_context(
                nc.sbuf_tensor("rstdbc", [128, T], F32))
            r1 = ln_scope.enter_context(nc.sbuf_tensor("r1", [1, T], F32))
            r2 = ln_scope.enter_context(nc.sbuf_tensor("r2", [1, T], F32))
            r3 = ln_scope.enter_context(nc.sbuf_tensor("r3", [1, T], F32))
            tmp1 = ln_scope.enter_context(
                nc.sbuf_tensor("tmp1", [128, T], F32))
            op("gpsimd", lambda e: e.memset(ones128[:, :], 1.0))
            op("gpsimd", lambda e: e.memset(ones1f[:, :], 1.0))
            op("gpsimd", lambda e: e.memset(zcol[:, :], 0.0))
            op("gpsimd", lambda e: e.memset(epsc[:, :], 1e-5))
            m_pad = op("gpsimd", lambda e: e.memset(padt[:, :], PAD),
                       inc="sw", amt=16)
            m_pad2 = op(
                "vector",
                lambda e: e.tensor_scalar_mul(
                    padt[:, :].rearrange("p (a v) -> p a v", v=4)[:, :, 0:1],
                    padt[:, :].rearrange("p (a v) -> p a v", v=4)[:, :, 0:1],
                    0.0),
                waits=[("sw", m_pad)], inc="sv")
            gl_init = op(
                "gpsimd",
                lambda e: e.dma_start(
                    glist.rearrange("(p a) v -> p (a v)", p=128),
                    padt[:, :]),
                waits=[("sw", m_pad), ("sv", m_pad2)], inc="sw", amt=16)
            for dst, src in ((bup_sb, bup), (bvp_sb, bvp), (bg2_sb, bg2p),
                             (bg1_sb, bg1p), (cwp_sb, cwp), (cbp_sb, cbp),
                             (ltri_sb, ltri), (tkw_sb, tkw), (tok_sb, tokid),
                             (srw_sb, srowid), (idxi, tki)):
                op("sync", lambda e, d=dst, s=src: e.dma_start(d[:, :], s),
                   inc="sx", amt=16)
            sx_small = cnt["sx"]

            def xdma_op(dc, sl, w):
                return op(
                    "sync",
                    lambda e, dc=dc, sl=sl: e.dma_start(
                        x_stage[:, sl:sl + T], xt[dc * 128:(dc + 1) * 128, :]),
                    waits=w, inc="sx", amt=16)

            xdma = {0: xdma_op(0, 0, []), 1: xdma_op(1, T, [])}
            cvt_done = {}
            mm_done = {}
            for dc in range(NDC):
                sl = (dc % 2) * T
                w = [("sx", xdma[dc])]
                if dc >= 1:
                    w.append(("st", mm_done[dc - 1]))
                op("scalar",
                   lambda e, sl=sl: e.copy(xbft[:, :], x_stage[:, sl:sl + T]),
                   waits=w)
                cvt_done[dc] = op(
                    "scalar",
                    lambda e, sl=sl: e.activation(
                        sqbf[:, :], x_stage[:, sl:sl + T], AF.Square),
                    inc="ss")
                for tb in range(NTB):
                    op("tensor",
                       lambda e, dc=dc, tb=tb: e.matmul(
                           psA[0:1, tb * TB:(tb + 1) * TB], ones128[:, :],
                           xbft[:, tb * TB:(tb + 1) * TB],
                           start=(dc == 0), stop=(dc == NDC - 1)),
                       waits=[("ss", cvt_done[dc])])
                for tb in range(NTB):
                    mm_done[dc] = op(
                        "tensor",
                        lambda e, dc=dc, tb=tb: e.matmul(
                            psB[0:1, tb * TB:(tb + 1) * TB], ones128[:, :],
                            sqbf[:, tb * TB:(tb + 1) * TB],
                            start=(dc == 0), stop=(dc == NDC - 1)),
                        inc="st" if tb == NTB - 1 else None)
                if dc + 2 < NDC:
                    xdma[dc + 2] = xdma_op(dc + 2, (dc % 2) * T,
                                           [("ss", cvt_done[dc])])

            stats_done = mm_done[NDC - 1]
            inv = 1.0 / D
            v_mean = op("vector",
                        lambda e: e.tensor_scalar_mul(r1[:, :], psA[0:1, :],
                                                      inv),
                        waits=[("st", stats_done)], inc="sv")
            op("vector", lambda e: e.tensor_mul(r2[:, :], r1[:, :], r1[:, :]))
            v_var = op("vector",
                       lambda e: e.scalar_tensor_tensor(
                           r2[:, :], psB[0:1, :], inv, r2[:, :],
                           ALU.mult, ALU.subtract),
                       inc="sv")
            s_sq = op("scalar",
                      lambda e: e.activation(r3[:, :], r2[:, :], AF.Sqrt,
                                             bias=epsc[0:1, :]),
                      waits=[("sv", v_var)], inc="ss")
            v_rec = op("vector", lambda e: e.reciprocal(r2[:, :], r3[:, :]),
                       waits=[("ss", s_sq)], inc="sv")
            bc_done = None
            for tb in range(NTB):
                op("tensor",
                   lambda e, tb=tb: e.matmul(
                       psA[:, tb * TB:(tb + 1) * TB], ones1f[:, :],
                       r1[0:1, tb * TB:(tb + 1) * TB], start=True, stop=True),
                   waits=[("sv", v_rec)])
            for tb in range(NTB):
                bc_done = op(
                    "tensor",
                    lambda e, tb=tb: e.matmul(
                        psB[:, tb * TB:(tb + 1) * TB], ones1f[:, :],
                        r2[0:1, tb * TB:(tb + 1) * TB], start=True, stop=True),
                    inc="st" if tb == NTB - 1 else None)
            s_mean = op("scalar", lambda e: e.copy(meanbc[:, :], psA[:, :]),
                        waits=[("st", bc_done)], inc="ss")
            v_rstd = op("vector", lambda e: e.tensor_copy(rstdbc[:, :],
                                                          psB[:, :]),
                        waits=[("st", bc_done)], inc="sv")

            w0 = [("ss", s_mean), ("sv", v_rstd)]
            xdma2 = {0: xdma_op(0, 0, w0), 1: xdma_op(1, T, w0)}
            bdone = {}
            edge_done = {}
            for dc in range(NDC):
                sl = (dc % 2) * T
                w = [("sx", xdma2[dc]), ("ss", s_mean), ("sv", v_rstd)]
                if dc >= 1:
                    w.append(("ss", edge_done[dc - 1]))
                op("vector",
                   lambda e, sl=sl: e.tensor_sub(
                       tmp1[:, :], x_stage[:, sl:sl + T], meanbc[:, :]),
                   waits=w)
                xn_dc = op(
                    "vector",
                    lambda e, dc=dc: e.tensor_mul(
                        xn[:, dc * T:(dc + 1) * T], tmp1[:, :], rstdbc[:, :]),
                    inc="sv")
                s_c1 = op("scalar",
                          lambda e, dc=dc: e.activation(
                              tmp1[:, :], xn[:, dc * T:(dc + 1) * T],
                              AF.Identity,
                              bias=cbp_sb[:, dc:dc + 1],
                              scale=cwp_sb[:, dc * 3 + 1:dc * 3 + 2]),
                          waits=[("sv", xn_dc), ("sx", sx_small)], inc="ss")
                op("vector",
                   lambda e, dc=dc: e.scalar_tensor_tensor(
                       tmp1[:, 1:T], xn[:, dc * T:dc * T + T - 1],
                       cwp_sb[:, dc * 3:dc * 3 + 1], tmp1[:, 1:T],
                       ALU.mult, ALU.add),
                   waits=[("ss", s_c1)])
                bdone[dc] = op(
                    "vector",
                    lambda e, dc=dc: e.scalar_tensor_tensor(
                        ybuf[:, dc * T:dc * T + T - 1],
                        xn[:, dc * T + 1:(dc + 1) * T],
                        cwp_sb[:, dc * 3 + 2:dc * 3 + 3], tmp1[:, 0:T - 1],
                        ALU.mult, ALU.add),
                    inc="sv")
                edge_done[dc] = op(
                    "scalar",
                    lambda e, dc=dc: e.copy(
                        ybuf[:, dc * T + T - 1:(dc + 1) * T],
                        tmp1[:, T - 1:T]),
                    waits=[("sv", bdone[dc])], inc="ss")
                if dc + 2 < NDC:
                    xdma2[dc + 2] = xdma_op(dc + 2, (dc % 2) * T,
                                            [("sv", bdone[dc])])
            xn_ready = cnt["sv"]
            convs_done = cnt["ss"]

            ln_scope.close()
            rt_scope = contextlib.ExitStack()
            ttile = rt_scope.enter_context(
                nc.sbuf_tensor("ttile", [128, 2 * T], BF16))
            idxf = rt_scope.enter_context(
                nc.sbuf_tensor("idxf", [128, 64], F32))
            macc = rt_scope.enter_context(
                nc.sbuf_tensor("macc", [128, 64], F32))
            mmask = rt_scope.enter_context(
                nc.sbuf_tensor("mmask", [128, 64], F32))
            mincl = rt_scope.enter_context(
                nc.sbuf_tensor("mincl", [128, 64], F32))
            mpos = rt_scope.enter_context(
                nc.sbuf_tensor("mpos", [128, 64], F32))
            mtmp = rt_scope.enter_context(
                nc.sbuf_tensor("mtmp", [128, 64], F32))
            ctot = rt_scope.enter_context(
                nc.sbuf_tensor("ctot", [128, 1], F32))
            addr_i = rt_scope.enter_context(
                nc.sbuf_tensor("addr_i", [128, 64], I32))
            val = rt_scope.enter_context(
                nc.sbuf_tensor("val", [128, 256], F32))

            conv_sp = op("sync", lambda e: e.dma_start(conv_h, ybuf[:, :]),
                         waits=[("sv", xn_ready), ("ss", convs_done)],
                         inc="sx", amt=16)

            xnfm_sp = op(
                "sync",
                lambda e: e.dma_start(
                    xn_fm.rearrange("(c p) t -> p c t", p=128),
                    xn[:, :].rearrange("p (c t) -> p c t", c=NDC)),
                waits=[("sv", xn_ready)], inc="sx", amt=16)
            tl_done = {}
            st_done = {}
            for tc in range(NDC):
                sl = (tc % 2) * T
                w = [("sx", xnfm_sp)]
                if tc >= 2:
                    w.append(("sx", st_done[tc - 2]))
                tl_done[tc] = op(
                    "scalar",
                    lambda e, tc=tc, sl=sl: e.dma_start_transpose(
                        ttile[:, sl:sl + T],
                        xn_fm[:, tc * 128:(tc + 1) * 128]),
                    waits=w, inc="sq", amt=16)
                st_done[tc] = op(
                    "sync",
                    lambda e, tc=tc, sl=sl: e.dma_start(
                        xn_tm[tc * 128:(tc + 1) * 128, :],
                        ttile[:, sl:sl + T]),
                    waits=[("sq", tl_done[tc])], inc="sx", amt=16)
            xn_tm_ready = st_done[NDC - 1]

            v_idx = op("vector", lambda e: e.tensor_copy(idxf[:, :],
                                                         idxi[:, :]),
                       waits=[("sx", sx_small), ("ss", convs_done),
                              ("sv", xn_ready)], inc="sv")
            op("vector", lambda e: e.memset(macc[:, :], 0.0))
            for e_i in range(NE):
                op("vector",
                   lambda e, e_i=e_i: e.tensor_single_scalar(
                       mmask[:, :], idxf[:, :], float(e_i), ALU.is_equal),
                   waits=[("sv", v_idx)])
                v_incl = op("vector",
                            lambda e: e.tensor_tensor_scan(
                                mincl[:, :], mmask[:, :],
                                zcol[:, :].broadcast_to((128, 64)), 0.0,
                                ALU.add, ALU.add),
                            inc="sv")
                rt_mm = op("tensor",
                           lambda e: e.matmul(psA[:, 0:1], ltri_sb[:, :],
                                              mincl[:, 63:64],
                                              start=True, stop=True),
                           waits=[("sv", v_incl)],
                           inc="st")
                s_ct = op("scalar",
                          lambda e: e.copy(ctot[:, :], psA[:, 0:1]),
                          waits=[("st", rt_mm)], inc="ss")
                op("vector",
                   lambda e: e.tensor_sub(mpos[:, :], mincl[:, :],
                                          mmask[:, :]))
                op("vector",
                   lambda e, e_i=e_i: e.tensor_scalar(
                       mpos[:, :], mpos[:, :], ctot[:, :], float(e_i * CAP),
                       ALU.add, ALU.add),
                   waits=[("ss", s_ct)])
                op("vector",
                   lambda e: e.tensor_mul(mtmp[:, :], mmask[:, :],
                                          mpos[:, :]))
                op("vector",
                   lambda e: e.tensor_add(macc[:, :], macc[:, :],
                                          mtmp[:, :]))
            v_addr = op("vector", lambda e: e.tensor_copy(addr_i[:, :],
                                                          macc[:, :]),
                        inc="sv")
            vz = op("vector", lambda e: e.memset(val[:, :], 0.0), inc="sv")
            vlast = vz
            for c, srcv in ((0, tok_sb), (1, srw_sb), (2, tkw_sb)):
                vlast = op(
                    "vector",
                    lambda e, c=c, srcv=srcv: e.tensor_copy(
                        val[:, :].rearrange(
                            "p (j c) -> p j c", c=4)[:, :, c:c + 1],
                        srcv[:, :].rearrange("p (j o) -> p j o", o=1)),
                    waits=[("sv", vz), ("sx", sx_small)], inc="sv")
            glsc = 0
            for j in range(64):
                glsc = op(
                    "gpsimd",
                    lambda e, j=j: e.indirect_dma_start(
                        glist,
                        bass.IndirectOffsetOnAxis(ap=addr_i[:, j:j + 1],
                                                  axis=0),
                        val[:, 4 * j:4 * j + 4], None),
                    waits=[("sv", v_addr), ("sv", vlast), ("sw", gl_init)],
                    inc="sw", amt=16)
            glist_ready = glsc
            if _DBG:
                op("sync", lambda e: e.dma_start(d_xn, xn[:, :]),
                   waits=[("sv", xn_ready)], inc="so", amt=16)
                op("sync", lambda e: e.dma_start(d_y1, ybuf[:, :]),
                   waits=[("ss", convs_done)], inc="so", amt=16)
            flush_block("ln")
            rt_scope.close()

        barrier_all()
        with nc.sbuf_tensor("wring", [128, 4 * 2048], BF16) as wring:
            slot_free = [("st", 0)] * 4
            blk_i = [0]

            def wblock_dma(src_ap, waits=()):
                j = blk_i[0] % 4
                blk_i[0] += 1
                dst = wring[:, j * 2048: j * 2048 + src_ap.shape[-1]]
                v = op("gpsimd",
                       lambda e, dst=dst, src=src_ap: e.dma_start(dst, src),
                       waits=[slot_free[j]] + list(waits), inc="sw", amt=16)
                return v, j

            def wflat_dma(row, off, width):
                j = blk_i[0] % 4
                blk_i[0] += 1
                dst = wring[:, j * 2048: j * 2048 + width]
                src = wmoe[row, off: off + 128 * width].rearrange(
                    "(p w) -> p w", w=width)
                v = op("gpsimd",
                       lambda e, dst=dst, src=src: e.dma_start(dst, src),
                       waits=[slot_free[j]], inc="sw", amt=16)
                return v, j

            with (
                nc.sbuf_tensor("glt", [128, 2 * 16], F32) as glt,
                nc.sbuf_tensor("gidx", [128, 2 * NCT], I32) as gidx,
                nc.sbuf_tensor("srow", [128, 2 * NCT], I32) as srow,
                nc.sbuf_tensor("xgst", [128, 2 * D], BF16) as xgst,
                nc.sbuf_tensor("xg_fm", [128, 2 * 16 * TB], BF16) as xg_fm,
                nc.sbuf_tensor("hmid", [128, NIC * TB], BF16) as hmid,
                nc.sbuf_tensor("silu_t", [128, 2 * TB], BF16) as silu_t,
                nc.sbuf_tensor("o_fm", [128, 2 * 16 * TB], BF16) as o_fm,
                nc.sbuf_tensor("o_tm", [128, 2 * D], BF16) as o_tm,
                nc.sbuf_tensor("o_sc", [128, 2 * D], BF16) as o_sc,
            ):
                z0 = op("gpsimd", lambda e: e.memset(xgst[:, :], 0.0),
                        inc="sw", amt=16)

                glt_l = {}
                drv_done = {}
                gst_done = {}
                tl16 = {}
                ofm_rd = {}
                otl = {}
                osc_done = {}
                sc_done = {}
                xg_used = {}
                hmid_used = {}
                ofm_evac = {}
                glt_used = {}

                def emit_gather(e_i):
                    par = e_i % 2
                    w = [("sw", glist_ready)]
                    if e_i >= 2:
                        w.append(("ss", glt_used.get(e_i - 2, 0)))
                    glt_l[e_i] = op(
                        "sync",
                        lambda e, e_i=e_i, par=par: e.dma_start(
                            glt[:, par * 16:(par + 1) * 16].rearrange(
                                "p (ct v) -> p ct v", v=4),
                            glist[e_i * CAP:(e_i + 1) * CAP, :].rearrange(
                                "(ct p) v -> p ct v", p=128)),
                        waits=w, inc="sx", amt=16)
                    g3 = glt[:, par * 16:(par + 1) * 16].rearrange(
                        "p (ct v) -> p ct v", v=4)
                    wv = [("sx", glt_l[e_i])]
                    if (e_i - 2, NCT - 1) in gst_done:
                        wv.append(("sw", gst_done[(e_i - 2, NCT - 1)]))
                    if (e_i - 2, NCT - 1) in sc_done:
                        wv.append(("sw", sc_done[(e_i - 2, NCT - 1)]))
                    op("vector",
                       lambda e, g3=g3, par=par: e.tensor_copy(
                           gidx[:, par * NCT:(par + 1) * NCT].rearrange(
                               "p (ct o) -> p ct o", o=1),
                           g3[:, :, 0:1]),
                       waits=wv)
                    drv_done[e_i] = op(
                        "vector",
                        lambda e, g3=g3, par=par: e.tensor_copy(
                            srow[:, par * NCT:(par + 1) * NCT].rearrange(
                                "p (ct o) -> p ct o", o=1),
                            g3[:, :, 1:2]),
                        inc="sv")
                    for ct in range(NCT):
                        sl = (ct % 2) * D
                        w = [("sv", drv_done[e_i]), ("sw", z0),
                             ("sx", xn_tm_ready)]
                        prev = (e_i, ct - 2) if ct >= 2 else (e_i - 1, ct + 2)
                        if prev in gst_done:
                            w.append(("sw", gst_done[prev]))
                        gg = op(
                            "gpsimd",
                            lambda e, sl=sl, par=par, ct=ct:
                            e.indirect_dma_start(
                                xgst[:, sl:sl + D], None,
                                xn_tm,
                                bass.IndirectOffsetOnAxis(
                                    ap=gidx[:, par * NCT + ct:
                                            par * NCT + ct + 1], axis=0)),
                            waits=w, inc="sw", amt=16)
                        gst_done[(e_i, ct)] = op(
                            "gpsimd",
                            lambda e, e_i=e_i, ct=ct, sl=sl: e.dma_start(
                                xg_h[e_i * CAP + ct * 128:
                                     e_i * CAP + (ct + 1) * 128, :],
                                xgst[:, sl:sl + D]),
                            waits=[("sw", gg)], inc="sw", amt=16)

                def emit_tloads(e_i):
                    par = e_i % 2
                    w = [("sw", gst_done[(e_i, NCT - 1)])]
                    if e_i >= 2:
                        w.append(("st", xg_used[e_i - 2]))
                    for hc in range(NDC):
                        tl16[e_i] = op(
                            "scalar",
                            lambda e, e_i=e_i, hc=hc, par=par:
                            e.dma_start_transpose(
                                xg_fm[:, par * 16 * TB + hc * TB:
                                      par * 16 * TB + (hc + 1) * TB],
                                xg_h[e_i * CAP:(e_i + 1) * CAP,
                                     hc * 128:(hc + 1) * 128]),
                            waits=w, inc="sq", amt=16)

                emit_gather(0)
                emit_tloads(0)
                emit_gather(1)

                bank_free = [("sv", 0)] * 8
                bslot = [0]

                def next_bank(extra_waits=()):
                    i = bslot[0] % 8
                    bslot[0] += 1
                    return i, [bank_free[i]] + list(extra_waits)

                for e_i in range(NE):
                    par = e_i % 2
                    g_of = par * 16 * TB
                    silu_done = {}
                    hmid_done = {}
                    for ic in range(NIC):
                        wg, jg = wflat_dma(e_i, ic * 2 * GU, 2048)
                        wu_, ju = wflat_dma(e_i, ic * 2 * GU + GU, 2048)
                        bi, bw = next_bank([("sw", wg), ("sq", tl16[e_i])])
                        g_last = None
                        for hc in range(NDC):
                            g_last = op(
                                "tensor",
                                lambda e, jg=jg, hc=hc, bi=bi, g_of=g_of:
                                e.matmul(
                                    bank_ap(bi),
                                    wring[:, jg * 2048 + hc * 128:
                                          jg * 2048 + (hc + 1) * 128],
                                    xg_fm[:, g_of + hc * TB:
                                          g_of + (hc + 1) * TB],
                                    start=(hc == 0), stop=(hc == NDC - 1)),
                                waits=bw,
                                inc="st" if hc == NDC - 1 else None)
                        slot_free[jg] = ("st", g_last)
                        sp = (ic % 2) * TB
                        silu_done[ic] = op(
                            "scalar",
                            lambda e, bi=bi, sp=sp: e.activation(
                                silu_t[:, sp:sp + TB], bank_ap(bi), AF.Silu),
                            waits=[("st", g_last),
                                   ("sv", hmid_done.get(ic - 2, 0))],
                            inc="ss")
                        bank_free[bi] = ("ss", silu_done[ic])
                        bi2, bw2 = next_bank([("sw", wu_)])
                        u_last = None
                        for hc in range(NDC):
                            u_last = op(
                                "tensor",
                                lambda e, ju=ju, hc=hc, bi2=bi2, g_of=g_of:
                                e.matmul(
                                    bank_ap(bi2),
                                    wring[:, ju * 2048 + hc * 128:
                                          ju * 2048 + (hc + 1) * 128],
                                    xg_fm[:, g_of + hc * TB:
                                          g_of + (hc + 1) * TB],
                                    start=(hc == 0), stop=(hc == NDC - 1)),
                                waits=bw2,
                                inc="st" if hc == NDC - 1 else None)
                        slot_free[ju] = ("st", u_last)
                        if ic == NIC - 1:
                            xg_used[e_i] = u_last
                        hmid_done[ic] = op(
                            "vector",
                            lambda e, bi2=bi2, ic=ic, sp=sp: e.tensor_mul(
                                hmid[:, ic * TB:(ic + 1) * TB],
                                bank_ap(bi2), silu_t[:, sp:sp + TB]),
                            waits=[("st", u_last), ("ss", silu_done[ic]),
                                   ("st", hmid_used.get(e_i - 1, 0))],
                            inc="sv")
                        bank_free[bi2] = ("sv", hmid_done[ic])
                    if e_i + 1 < NE:
                        emit_tloads(e_i + 1)
                    for hc in range(NDC):
                        wd, jd = wflat_dma(e_i, 16 * GU + hc * DD, 1024)
                        bi, bw = next_bank([("sw", wd)])
                        o_last = None
                        for ic in range(NIC):
                            o_last = op(
                                "tensor",
                                lambda e, jd=jd, ic=ic, bi=bi: e.matmul(
                                    bank_ap(bi),
                                    wring[:, jd * 2048 + ic * 128:
                                          jd * 2048 + (ic + 1) * 128],
                                    hmid[:, ic * TB:(ic + 1) * TB],
                                    start=(ic == 0), stop=(ic == NIC - 1)),
                                waits=bw + [("sv", hmid_done[ic])],
                                inc="st" if ic == NIC - 1 else None)
                        slot_free[jd] = ("st", o_last)
                        if hc == NDC - 1:
                            hmid_used[e_i] = o_last
                        w = [("st", o_last)]
                        if e_i >= 2:
                            w.append(("sx", ofm_rd[e_i - 2]))
                        ofm_evac[e_i] = op(
                            "scalar",
                            lambda e, bi=bi, hc=hc, g_of=g_of: e.copy(
                                o_fm[:, g_of + hc * TB:g_of + (hc + 1) * TB],
                                bank_ap(bi)),
                            waits=w, inc="ss")
                        bank_free[bi] = ("ss", ofm_evac[e_i])
                    ofm_rd[e_i] = op(
                        "sync",
                        lambda e, e_i=e_i, g_of=g_of: e.dma_start(
                            o_h[e_i].rearrange("(hc p) c -> p hc c", p=128),
                            o_fm[:, g_of:g_of + 16 * TB].rearrange(
                                "p (hc c) -> p hc c", hc=16)),
                        waits=[("ss", ofm_evac[e_i])], inc="sx", amt=16)
                    for ct in range(NCT):
                        sl = (ct % 2) * D
                        w = [("sx", ofm_rd[e_i])]
                        prev = (e_i, ct - 2) if ct >= 2 else (e_i - 1, ct + 2)
                        if prev in osc_done:
                            w.append(("ss", osc_done[prev]))
                        otl[(e_i, ct)] = op(
                            "scalar",
                            lambda e, e_i=e_i, ct=ct, sl=sl:
                            e.dma_start_transpose(
                                o_tm[:, sl:sl + D],
                                o_h[e_i][:, ct * 128:(ct + 1) * 128]),
                            waits=w, inc="sq", amt=16)
                        wsc = [("sq", otl[(e_i, ct)])]
                        if prev in sc_done:
                            wsc.append(("sw", sc_done[prev]))
                        wc = par * 16 + ct * 4 + 2
                        osc_done[(e_i, ct)] = op(
                            "scalar",
                            lambda e, sl=sl, wc=wc: e.mul(
                                o_sc[:, sl:sl + D], o_tm[:, sl:sl + D],
                                glt[:, wc:wc + 1]),
                            waits=wsc, inc="ss")
                        if ct == NCT - 1:
                            glt_used[e_i] = osc_done[(e_i, ct)]
                        sc_done[(e_i, ct)] = op(
                            "gpsimd",
                            lambda e, sl=sl, par=par, ct=ct:
                            e.indirect_dma_start(
                                y4,
                                bass.IndirectOffsetOnAxis(
                                    ap=srow[:, par * NCT + ct:
                                            par * NCT + ct + 1], axis=0),
                                o_sc[:, sl:sl + D], None),
                            waits=[("ss", osc_done[(e_i, ct)])],
                            inc="sw", amt=16)
                    if e_i + 2 < NE:
                        emit_gather(e_i + 2)
                scatters_done = cnt["sw"]
                flush_block("moe")

            barrier_all()
            with nc.sbuf_tensor("gobuf", [128, NRC * T], BF16) as gobuf:
                at_scope = contextlib.ExitStack()
                wg1_bf = at_scope.enter_context(
                    nc.sbuf_tensor("wg1_bf", [128, 8192], BF16))
                h_all = at_scope.enter_context(
                    nc.sbuf_tensor("h_all", [128, NDHC * T], BF16))
                gates = at_scope.enter_context(
                    nc.sbuf_tensor("gates", [128, T], F32))
                ug = at_scope.enter_context(
                    nc.sbuf_tensor("ug", [128, T], F32))
                vg = at_scope.enter_context(
                    nc.sbuf_tensor("vg", [128, T], F32))
                d_wg1 = op("gpsimd", lambda e: e.dma_start(wg1_bf[:, :], wg1d),
                           inc="sw", amt=16)
                h_done = {}
                relu_done = {}
                for dhc in range(NDHC):
                    ps = ps_of[dhc % 2]
                    for dc in range(NDC):
                        for tb in range(NTB):
                            h_done[dhc] = op(
                                "tensor",
                                lambda e, ps=ps, dc=dc, dhc=dhc, tb=tb:
                                e.matmul(
                                    ps[:, tb * TB:(tb + 1) * TB],
                                    wg1_bf[:, dc * 512 + dhc * 128:
                                           dc * 512 + (dhc + 1) * 128],
                                    xn[:, dc * T + tb * TB:
                                       dc * T + (tb + 1) * TB],
                                    start=(dc == 0), stop=(dc == NDC - 1)),
                                waits=[("sw", d_wg1),
                                       ("ss", relu_done.get(dhc - 2, 0))],
                                inc="st" if (dc == NDC - 1 and tb == NTB - 1)
                                else None)
                    relu_done[dhc] = op(
                        "scalar",
                        lambda e, ps=ps, dhc=dhc: e.activation(
                            h_all[:, dhc * T:(dhc + 1) * T], ps[:, :], AF.Relu,
                            bias=bg1_sb[:, dhc:dhc + 1]),
                        waits=[("st", h_done[dhc])], inc="ss")
                h_ready = relu_done[NDHC - 1]

                psA_free = ("ss", relu_done.get(NDHC - 2, 0))
                psB_free = ("ss", h_ready)
                go_done = {}
                for rc in range(NRC):
                    v2, jg2 = wblock_dma(wg2d[rc])
                    vu, ju_ = wblock_dma(wud[rc])
                    vv, jv_ = wblock_dma(wvd[rc])

                    gmm = None
                    for dhc in range(NDHC):
                        for tb in range(NTB):
                            gmm = op(
                                "tensor",
                                lambda e, jg2=jg2, dhc=dhc, tb=tb: e.matmul(
                                    psB[:, tb * TB:(tb + 1) * TB],
                                    wring[:, jg2 * 2048 + dhc * 128:
                                          jg2 * 2048 + (dhc + 1) * 128],
                                    h_all[:, dhc * T + tb * TB:
                                          dhc * T + (tb + 1) * TB],
                                    start=(dhc == 0), stop=(dhc == NDHC - 1)),
                                waits=[("sw", v2), ("ss", h_ready), psB_free],
                                inc="st" if (dhc == NDHC - 1 and tb == NTB - 1)
                                else None)
                    slot_free[jg2] = ("st", gmm)
                    s_gate = op(
                        "scalar",
                        lambda e, rc=rc: e.activation(
                            gates[:, :], psB[:, :], AF.Sigmoid,
                            bias=bg2_sb[:, rc:rc + 1]),
                        waits=[("st", gmm)], inc="ss")
                    psB_free = ("ss", s_gate)
                    umm = None
                    for dc in range(NDC):
                        for tb in range(NTB):
                            umm = op(
                                "tensor",
                                lambda e, ju_=ju_, dc=dc, tb=tb: e.matmul(
                                    psA[:, tb * TB:(tb + 1) * TB],
                                    wring[:, ju_ * 2048 + dc * 128:
                                          ju_ * 2048 + (dc + 1) * 128],
                                    xn[:, dc * T + tb * TB:
                                       dc * T + (tb + 1) * TB],
                                    start=(dc == 0), stop=(dc == NDC - 1)),
                                waits=[("sw", vu), psA_free],
                                inc="st" if (dc == NDC - 1 and tb == NTB - 1)
                                else None)
                    slot_free[ju_] = ("st", umm)
                    v_ug = op("vector",
                              lambda e, rc=rc: e.scalar_tensor_tensor(
                                  ug[:, :], psA[:, :], bup_sb[:, rc:rc + 1],
                                  gates[:, :], ALU.add, ALU.mult),
                              waits=[("st", umm), ("ss", s_gate)], inc="sv")
                    psA_free = ("sv", v_ug)
                    vmm = None
                    for dc in range(NDC):
                        for tb in range(NTB):
                            vmm = op(
                                "tensor",
                                lambda e, jv_=jv_, dc=dc, tb=tb: e.matmul(
                                    psB[:, tb * TB:(tb + 1) * TB],
                                    wring[:, jv_ * 2048 + dc * 128:
                                          jv_ * 2048 + (dc + 1) * 128],
                                    xn[:, dc * T + tb * TB:
                                       dc * T + (tb + 1) * TB],
                                    start=(dc == 0), stop=(dc == NDC - 1)),
                                waits=[("sw", vv), psB_free],
                                inc="st" if (dc == NDC - 1 and tb == NTB - 1)
                                else None)
                    slot_free[jv_] = ("st", vmm)
                    v_vg = op("vector",
                              lambda e, rc=rc: e.scalar_tensor_tensor(
                                  vg[:, :], psB[:, :], bvp_sb[:, rc:rc + 1],
                                  gates[:, :], ALU.add, ALU.mult),
                              waits=[("st", vmm)], inc="sv")
                    psB_free = ("sv", v_vg)
                    op("vector",
                       lambda e: e.tensor_tensor_scan(
                           ug[:, :], ug[:, :],
                           zcol[:, :].broadcast_to((128, T)), 0.0,
                           ALU.add, ALU.add))
                    go_done[rc] = op(
                        "vector",
                        lambda e, rc=rc: e.tensor_mul(
                            gobuf[:, rc * T:(rc + 1) * T], ug[:, :],
                            vg[:, :]),
                        inc="sv")
                go_ready = go_done[NRC - 1]
                if _DBG:
                    op("sync", lambda e: e.dma_start(d_go, gobuf[:, :]),
                       waits=[("sv", go_ready)], inc="so", amt=16)
                flush_block("attn")
                at_scope.close()

                barrier_all()
                with (
                    nc.sbuf_tensor("x_stage2", [128, T], F32) as x_stage2,
                    nc.sbuf_tensor("cstr", [128, T], BF16) as cstr,
                    nc.sbuf_tensor("ytp", [128, NSLOT], BF16) as ytp,
                    nc.sbuf_tensor("ostg", [128, T], F32) as ostg,
                    nc.sbuf_tensor("ftmp", [128, T], F32) as ftmp,
                    nc.sbuf_tensor("ftm2", [128, T], F32) as ftm2,
                ):
                    xl = {}
                    cl = {}
                    yl = {}
                    out_done = {}
                    vsum = {}
                    for dc in range(NDC):
                        w = [] if dc < 1 else [("sv", vsum[dc - 1])]
                        if dc >= 1:
                            out_done[dc - 1] = op(
                                "sync",
                                lambda e, dc=dc: e.dma_start(
                                    outt[(dc - 1) * 128:dc * 128, :],
                                    ostg[:, :]),
                                waits=w, inc="so", amt=16)
                        xl[dc] = op("sync",
                                    lambda e, dc=dc: e.dma_start(
                                        x_stage2[:, :],
                                        xt[dc * 128:(dc + 1) * 128, :]),
                                    waits=w, inc="sx", amt=16)
                        cl[dc] = op("sync",
                                    lambda e, dc=dc: e.dma_start(
                                        cstr[:, :],
                                        conv_h[:, dc * T:(dc + 1) * T]),
                                    inc="sx", amt=16)
                        yl[dc] = op(
                            "scalar",
                            lambda e, dc=dc: e.dma_start_transpose(
                                ytp[:, :],
                                y4[0:NSLOT, dc * 128:(dc + 1) * 128]),
                            waits=(w + [("sw", scatters_done)]),
                            inc="sq", amt=16)
                        vw, jw = wblock_dma(wud[dc])
                        ps = ps_of[dc % 2]
                        uv_last = None
                        for rc in range(NRC):
                            for tb in range(NTB):
                                uv_last = op(
                                    "tensor",
                                    lambda e, ps=ps, jw=jw, rc=rc, tb=tb:
                                    e.matmul(
                                        ps[:, tb * TB:(tb + 1) * TB],
                                        wring[:, jw * 2048 + rc * 128:
                                              jw * 2048 + (rc + 1) * 128],
                                        gobuf[:, rc * T + tb * TB:
                                              rc * T + (tb + 1) * TB],
                                        start=(rc == 0),
                                        stop=(rc == NRC - 1)),
                                    waits=[("sw", vw),
                                           ("sv", vsum.get(dc - 2, 0))],
                                    inc="st" if (rc == NRC - 1 and
                                                 tb == NTB - 1) else None)
                        slot_free[jw] = ("st", uv_last)
                        op("vector",
                           lambda e, ps=ps: e.tensor_add(
                               ftmp[:, :], ps[:, :], x_stage2[:, :]),
                           waits=[("st", uv_last), ("sx", xl[dc]),
                                  ("sx", cl[dc]), ("sq", yl[dc]),
                                  ("so", out_done.get(dc - 1, 0))])
                        op("vector",
                           lambda e: e.tensor_add(
                               ftmp[:, :], ftmp[:, :], cstr[:, :]))
                        op("vector",
                           lambda e: e.tensor_add(
                               ftm2[:, :], ytp[:, 0:T], ytp[:, T:2 * T]))
                        op("vector",
                           lambda e: e.tensor_add(ftmp[:, :], ftmp[:, :],
                                                  ftm2[:, :]))
                        op("vector",
                           lambda e: e.tensor_add(
                               ftm2[:, :], ytp[:, 2 * T:3 * T],
                               ytp[:, 3 * T:4 * T]))
                        vsum[dc] = op(
                            "vector",
                            lambda e: e.tensor_add(
                                ostg[:, :], ftmp[:, :], ftm2[:, :]),
                            inc="sv")
                    out_done[NDC - 1] = op(
                        "sync",
                        lambda e: e.dma_start(
                            outt[(NDC - 1) * 128:NDC * 128, :], ostg[:, :]),
                        waits=[("sv", vsum[NDC - 1])], inc="so", amt=16)
                    wait_only("sync", [("so", cnt["so"])])
                    flush_block("final")

    return nc


def _pack_t4(a):
    n = a.shape[0] // 128
    c = a.shape[1] // 128
    return np.ascontiguousarray(
        a.reshape(n, 128, c, 128).transpose(2, 1, 0, 3).reshape(
            c, 128, n * 128))


def kernel(L0_attn_x, L0_attn_Wu, L0_attn_bu, L0_attn_Wv, L0_attn_bv,
           L0_attn_Wg1, L0_attn_bg1, L0_attn_Wg2, L0_attn_bg2,
           L0_attn_conv_w, L0_attn_conv_b,
           L0_ffn_gate_weight, L0_ffn_up_weight, L0_ffn_down_weight,
           L0_ffn_topk_idx, L0_ffn_topk_weight):
    x = np.asarray(L0_attn_x, dtype=np.float32)
    Wu = np.asarray(L0_attn_Wu, dtype=np.float32)
    Wv = np.asarray(L0_attn_Wv, dtype=np.float32)
    Wg1 = np.asarray(L0_attn_Wg1, dtype=np.float32)
    Wg2 = np.asarray(L0_attn_Wg2, dtype=np.float32)
    bu = np.asarray(L0_attn_bu, dtype=np.float32)
    bv = np.asarray(L0_attn_bv, dtype=np.float32)
    bg1 = np.asarray(L0_attn_bg1, dtype=np.float32)
    bg2 = np.asarray(L0_attn_bg2, dtype=np.float32)
    convw = np.asarray(L0_attn_conv_w, dtype=np.float32)
    convb = np.asarray(L0_attn_conv_b, dtype=np.float32)
    gw = np.asarray(L0_ffn_gate_weight, dtype=np.float32)
    uw = np.asarray(L0_ffn_up_weight, dtype=np.float32)
    dw = np.asarray(L0_ffn_down_weight, dtype=np.float32)
    tidx = np.asarray(L0_ffn_topk_idx, dtype=np.int32)
    tw = np.asarray(L0_ffn_topk_weight, dtype=np.float32)

    B = x.shape[0]

    wud = _pack_t4(np.ascontiguousarray(Wu.T))
    wvd = _pack_t4(np.ascontiguousarray(Wv.T))
    wg2d = _pack_t4(np.ascontiguousarray(Wg2.T))
    wg1d = np.ascontiguousarray(
        Wg1.T.reshape(16, 128, 512).transpose(1, 0, 2).reshape(128, 8192))
    moe_blocks = []
    for e in range(NE):
        gt = _pack_t4(np.ascontiguousarray(gw[e].T))
        ut = _pack_t4(np.ascontiguousarray(uw[e].T))
        dt = _pack_t4(np.ascontiguousarray(dw[e].T))
        parts = []
        for ic in range(NIC):
            parts.append(gt[ic].ravel())
            parts.append(ut[ic].ravel())
        for hc in range(NDC):
            parts.append(dt[hc].ravel())
        moe_blocks.append(np.concatenate(parts))
    wmoe_np = np.ascontiguousarray(np.stack(moe_blocks))
    del moe_blocks

    bup_np = np.ascontiguousarray(bu.reshape(16, 128).T)
    bvp_np = np.ascontiguousarray(bv.reshape(16, 128).T)
    bg2_np = np.ascontiguousarray(bg2.reshape(16, 128).T)
    bg1_np = np.ascontiguousarray(bg1.reshape(4, 128).T)
    cwp_np = np.ascontiguousarray(
        convw[:, 0, :].reshape(16, 128, 3).transpose(1, 0, 2).reshape(128, 48))
    cbp_np = np.ascontiguousarray(convb.reshape(16, 128).T)

    s_all = np.arange(NSLOT)
    t_all = s_all // NK
    k_all = s_all % NK
    tok_np = np.ascontiguousarray(t_all.astype(np.float32).reshape(128, 64))
    srow_np = np.ascontiguousarray(
        (k_all * T + t_all).astype(np.float32).reshape(128, 64))
    ltri_np = np.ascontiguousarray(np.triu(np.ones((128, 128), np.float32), 1))

    nc = _build_program()
    in_maps = []
    for b in range(B):
        ts = tidx[b * T:(b + 1) * T].reshape(128, 64)
        ws = tw[b * T:(b + 1) * T].reshape(128, 64)
        in_maps.append({
            "xt": np.ascontiguousarray(x[b].T),
            "wud": wud, "wvd": wvd, "wg2d": wg2d, "wg1d": wg1d,
            "wmoe": wmoe_np,
            "bup": bup_np, "bvp": bvp_np, "bg2p": bg2_np, "bg1p": bg1_np,
            "cwp": cwp_np, "cbp": cbp_np,
            "tki": np.ascontiguousarray(ts),
            "tkw": np.ascontiguousarray(ws),
            "tokid": tok_np, "srowid": srow_np, "ltri": ltri_np,
        })
    res = run_bass_kernel_spmd(nc, in_maps, core_ids=list(range(B)))
    if _DBG:
        kernel._dbg = res.results
    out = np.empty((B, T, D), dtype=np.float32)
    for b in range(B):
        out[b] = np.asarray(res.results[b]["outt"]).T
    return out
